# version 01
# baseline (speedup 1.0000x reference)
import contextlib
import math
import numpy as np
import ml_dtypes

import concourse.bass as bass
import concourse.mybir as mybir
from concourse.bass_utils import run_bass_kernel_spmd
from concourse.alu_op_type import AluOpType as ALU

F32 = mybir.dt.float32
BF16 = mybir.dt.bfloat16
I32 = mybir.dt.int32
AF = mybir.ActivationFunctionType
AX = mybir.AxisListType
BB = ml_dtypes.bfloat16

D = 2048
KT = 16
NH = 16
HD = 128
LAT = 512
MLP0 = 16384
MLP1 = 8192
LORA_R = 16
B = 2
S = 8192
NCORES = 8
TCORE = 2048
EPS = 1e-5
GELU_C = 0.7978845608028654
GELU_A = 0.044715
RSQD = 1.0 / math.sqrt(128.0)
DBG_L1FFN = None


ENGS = ("pe", "dve", "act", "pool", "qs", "qa", "qp")
QSEMS = 8
STREAM_OF = {"pe": "tensor", "dve": "vector", "act": "scalar", "qa": "scalar",
             "pool": "gpsimd", "qp": "gpsimd", "qs": "sync"}


class _Item:
    __slots__ = ("eng", "fn", "reads", "writes", "idx", "tag", "inc")

    def __init__(self, eng, fn, reads, writes, idx, tag):
        self.eng, self.fn, self.reads, self.writes = eng, fn, reads, writes
        self.idx, self.tag, self.inc = idx, tag, False


class Prog:
    def __init__(self, nc):
        self.nc = nc
        self.plan = []
        self.counts = {e: 0 for e in ENGS}
        self.pe_strict = False
        self.self_sync = True

    def op(self, eng, fn, reads=(), writes=(), tag=""):
        it = _Item(eng, fn, tuple(reads), tuple(writes), self.counts[eng], tag)
        self.counts[eng] += 1
        self.plan.append(it)
        return it

    def dma(self, queue, out_ap, in_ap, reads=(), writes=(), tag="", transpose=False):
        def fn(eng, o=out_ap, i=in_ap, tr=transpose):
            return eng.dma_start(o, i, transpose=True) if tr else eng.dma_start(o, i)
        it = _Item(queue, fn, tuple(reads), tuple(writes), self.counts[queue], tag)
        self.counts[queue] += 1
        self.plan.append(it)
        return it

    def barrier_regions(self):
        rg = set()
        for it in self.plan:
            rg.update(it.reads)
            rg.update(it.writes)
        return rg

    def _resolve(self):
        last_write, readers, waits = {}, {}, []
        for it in self.plan:
            w = {}
            st = STREAM_OF[it.eng]

            def add_edge(peng, pidx, w=w, it=it, st=st):
                if STREAM_OF[peng] == st:
                    if peng == "pe" and it.eng == "pe" and not self.pe_strict:
                        return
                    if peng in ("qs", "qa", "qp") and peng == it.eng:
                        return
                    if not self.self_sync and peng == it.eng:
                        return
                key = peng if peng not in ("qs", "qa", "qp") else \
                    f"{peng}{pidx % QSEMS}"
                if w.get(key, -1) < pidx:
                    w[key] = pidx

            for r in it.reads:
                for lw in last_write.get(r, ()):
                    add_edge(*lw)
            for r in it.writes:
                for lw in last_write.get(r, ()):
                    add_edge(*lw)
                for peng, pidx in readers.get(r, {}).items():
                    add_edge(peng, pidx)
            waits.append(w)
            for r in it.reads:
                d = readers.setdefault(r, {})
                if d.get(it.eng, -1) < it.idx:
                    d[it.eng] = it.idx
            for r in it.writes:
                if readers.get(r):
                    last_write[r] = [(it.eng, it.idx)]
                else:
                    last_write.setdefault(r, []).append((it.eng, it.idx))
                readers[r] = {}
        needed = {e: set() for e in ("pe", "dve", "act", "pool")}
        for w in waits:
            for key, pidx in w.items():
                if key in needed:
                    needed[key].add(pidx)
        ranks = {e: {idx: r + 1 for r, idx in enumerate(sorted(s))}
                 for e, s in needed.items()}
        for it in self.plan:
            if it.eng in ranks and it.idx in ranks[it.eng]:
                it.inc = True
        streams = {s: [] for s in ("tensor", "vector", "scalar", "gpsimd", "sync")}
        hwm = {}
        for it, w in zip(self.plan, waits):
            st = STREAM_OF[it.eng]
            ws = []
            if it.eng in ("qs", "qa", "qp") and it.idx >= QSEMS:
                ws.append((f"{it.eng}{it.idx % QSEMS}",
                           16 * (it.idx // QSEMS)))
            for key, pidx in sorted(w.items()):
                if key in ranks:
                    val = ranks[key][pidx]
                else:
                    val = 16 * (pidx // QSEMS + 1)
                hk = (st, key)
                if hwm.get(hk, 0) >= val:
                    continue
                hwm[hk] = val
                ws.append((key, val))
            streams[st].append((it, ws))
        return streams

    def verify(self):
        streams = self._resolve()
        wmap = {}
        order = {}
        for sname, lst in streams.items():
            for pos, (it, ws) in enumerate(lst):
                wmap[id(it)] = ws
                order[id(it)] = (sname, pos)
        clocks = {}
        stream_clock = {s: {} for s in
                        ("tensor", "vector", "scalar", "gpsimd", "sync")}
        ranks = {}
        cnt = {e: 0 for e in ("pe", "dve", "act", "pool")}
        rank_to_idx = {e: {} for e in ("pe", "dve", "act", "pool")}
        for it in self.plan:
            if it.eng in cnt and it.inc:
                cnt[it.eng] += 1
                ranks[id(it)] = (it.eng, cnt[it.eng])
                rank_to_idx[it.eng][cnt[it.eng]] = it.idx
            elif it.eng in ("qs", "qa", "qp"):
                ranks[id(it)] = (f"{it.eng}{it.idx % QSEMS}",
                                 16 * (it.idx // QSEMS + 1))
        done_clock_of = {}

        def merge(dst, src):
            for k, v in src.items():
                if dst.get(k, -1) < v:
                    dst[k] = v

        ev_to_item = {}
        for it in self.plan:
            if id(it) in ranks:
                ev_to_item[ranks[id(it)]] = it
        for it in self.plan:
            sname = STREAM_OF[it.eng]
            c = dict(stream_clock[sname])
            for key, val in wmap[id(it)]:
                if c.get(key, -1) < val:
                    c[key] = val
                tgt = ev_to_item.get((key, val))
                if tgt is not None:
                    merge(c, done_clock_of[id(tgt)])
            clocks[id(it)] = c
            stream_clock[sname] = c
            d = dict(c)
            if id(it) in ranks:
                k, v = ranks[id(it)]
                if d.get(k, -1) < v:
                    d[k] = v
            done_clock_of[id(it)] = d
        last_write, readers = {}, {}
        bad = []

        def check(prod, cons, why):
            if prod.eng == cons.eng and STREAM_OF[prod.eng] == STREAM_OF[cons.eng]:
                if prod.eng == cons.eng and prod.eng in ("qs", "qa", "qp"):
                    return
                if prod.eng == "pe" and cons.eng == "pe":
                    return
            if STREAM_OF[prod.eng] == STREAM_OF[cons.eng] and \
                    prod.eng in ("qs", "qa", "qp") and cons.eng not in ("qs", "qa", "qp"):
                pass
            if prod.eng in ("pe", "dve", "act", "pool"):
                v = clocks[id(cons)].get(prod.eng, -1)
                cov = rank_to_idx[prod.eng].get(v, -1)
                if cov < prod.idx:
                    bad.append((why, prod.eng, prod.idx, cons.eng, cons.idx,
                                f"eng-clock rank {v} covers idx {cov}"))
                return
            k, v = ranks[id(prod)]
            if clocks[id(cons)].get(k, -1) < v:
                bad.append((why, prod.eng, prod.idx, cons.eng, cons.idx,
                            f"needs {k}>={v} has {clocks[id(cons)].get(k, -1)}"))

        for it in self.plan:
            for rg in it.reads:
                for lw in last_write.get(rg, ()):
                    if lw is not it:
                        check(lw, it, f"RAW:{rg}")
            for rg in it.writes:
                for lw in last_write.get(rg, ()):
                    if lw is not it:
                        check(lw, it, f"WAW:{rg}")
                for rd in readers.get(rg, ()):
                    if rd is not it:
                        check(rd, it, f"WAR:{rg}")
            for rg in it.reads:
                readers.setdefault(rg, []).append(it)
            for rg in it.writes:
                if readers.get(rg):
                    last_write[rg] = [it]
                else:
                    last_write.setdefault(rg, []).append(it)
                readers[rg] = []
            if len(bad) > 10:
                break
        return bad

    def emit(self, tail=None):
        streams = self._resolve()
        nc = self.nc
        self._stack = contextlib.ExitStack()
        sems = {}
        for e in ("pe", "dve", "act", "pool"):
            sems[e] = self._stack.enter_context(nc.semaphore(f"sem_{e}"))
        for q in ("qs", "qa", "qp"):
            for s in range(QSEMS):
                sems[f"{q}{s}"] = self._stack.enter_context(
                    nc.semaphore(f"sem_{q}{s}"))
        self.sems = sems

        def runner(sname):
            def body(eng):
                for it, ws in streams[sname]:
                    for peng, val in ws:
                        eng.wait_ge(sems[peng], val)
                    inst = it.fn(eng)
                    if it.eng in ("qs", "qa", "qp"):
                        inst.then_inc(sems[f"{it.eng}{it.idx % QSEMS}"], 16)
                    elif it.inc:
                        inst.then_inc(sems[it.eng], 1)
                if tail and sname in tail:
                    tail[sname](eng, sems)
            return body

        with nc.Block() as blk:
            blk.tensor(runner("tensor"))
            blk.vector(runner("vector"))
            blk.scalar(runner("scalar"))
            blk.gpsimd(runner("gpsimd"))
            blk.sync(runner("sync"))
        return sems

    def dma_counts(self):
        return {q: self.counts[q] for q in ("qs", "qa", "qp")}



class Arena:
    SLOT = 512

    def __init__(self, handle, nbytes):
        self.h = handle
        self.nbytes = nbytes
        self.off = 0
        self.marks = []

    def carve(self, n_elems, dtype, align=64):
        esz = {F32: 4, BF16: 2, I32: 4}[dtype]
        nb = n_elems * esz
        off = (self.off + align - 1) // align * align
        assert off + nb <= self.nbytes, f"arena overflow: {off}+{nb}>{self.nbytes}"
        self.off = off + nb
        ap = self.h[:, :].bitcast(dtype)[:, off // esz: off // esz + n_elems]
        regions = tuple(f"A{s}" for s in range(off // self.SLOT,
                                               (off + nb - 1) // self.SLOT + 1))
        return ap, regions

    def push(self):
        self.marks.append(self.off)

    def pop(self):
        self.off = self.marks.pop()


class Tile:
    __slots__ = ("ap", "rg")

    def __init__(self, ap, rg):
        self.ap = ap
        self.rg = tuple(rg)

    def __getitem__(self, sl):
        return Tile(self.ap[sl], self.rg)

    def view(self, ap):
        return Tile(ap, self.rg)


def carve(ar, n, dtype, align=64):
    ap, rg = ar.carve(n, dtype, align)
    return Tile(ap, rg)



class E:

    def __init__(self, prog):
        self.P = prog

    def tt(self, out, a, b, op, extra_reads=()):
        self.P.op("dve", lambda e, o=out.ap, x=a.ap, y=b.ap, op=op:
                  e.tensor_tensor(o, x, y, op),
                  reads=a.rg + b.rg + tuple(extra_reads), writes=out.rg)

    def ts(self, out, a, s1, s2, op0, op1=None, engine="dve"):
        r = a.rg + (s1.rg if isinstance(s1, Tile) else ()) + \
            (s2.rg if isinstance(s2, Tile) else ())
        s1v = s1.ap if isinstance(s1, Tile) else s1
        s2v = s2.ap if isinstance(s2, Tile) else s2
        if op1 is None:
            self.P.op(engine, lambda e, o=out.ap, x=a.ap, s=s1v, op=op0:
                      e.tensor_scalar(o, x, s, None, op),
                      reads=r, writes=out.rg)
        else:
            self.P.op(engine, lambda e, o=out.ap, x=a.ap, sa=s1v, sb=s2v,
                      opa=op0, opb=op1:
                      e.tensor_scalar(o, x, sa, sb, opa, opb),
                      reads=r, writes=out.rg)

    def stt(self, out, a, s, b, op0, op1):
        sv = s.ap if isinstance(s, Tile) else s
        r = a.rg + b.rg + (s.rg if isinstance(s, Tile) else ())
        self.P.op("dve", lambda e, o=out.ap, x=a.ap, sc=sv, y=b.ap, oa=op0, ob=op1:
                  e.scalar_tensor_tensor(o, x, sc, y, oa, ob),
                  reads=r, writes=out.rg)

    def copy(self, out, a, engine="dve"):
        self.P.op(engine, lambda e, o=out.ap, x=a.ap: e.tensor_copy(o, x),
                  reads=a.rg, writes=out.rg)

    def red(self, out, a, op, negate=None):
        self.P.op("dve", lambda e, o=out.ap, x=a.ap, op=op, n=negate:
                  e.tensor_reduce(o, x, AX.X, op, negate=n),
                  reads=a.rg, writes=out.rg)

    def recip(self, out, a):
        self.P.op("dve", lambda e, o=out.ap, x=a.ap: e.reciprocal(o, x),
                  reads=a.rg, writes=out.rg)

    def memset(self, out, v, engine="dve"):
        self.P.op(engine, lambda e, o=out.ap, v=v: e.memset(o, v),
                  reads=(), writes=out.rg)

    def act(self, out, a, func, bias=0.0, scale=1.0):
        bv = bias.ap if isinstance(bias, Tile) else bias
        r = a.rg + (bias.rg if isinstance(bias, Tile) else ())
        self.P.op("act", lambda e, o=out.ap, x=a.ap, f=func, b=bv, s=scale:
                  e.activation(o, x, f, bias=b, scale=s),
                  reads=r, writes=out.rg)

    def mm(self, out, lhsT, rhs, start, stop):
        self.P.op("pe", lambda e, o=out.ap, l=lhsT.ap, r=rhs.ap, st=start, sp=stop:
                  e.matmul(o, l, r, start=st, stop=sp),
                  reads=lhsT.rg + rhs.rg, writes=out.rg)

    def trans(self, out, a, ident):
        self.P.op("pe", lambda e, o=out.ap, x=a.ap, i=ident.ap:
                  e.transpose(o, x, i),
                  reads=a.rg + ident.rg, writes=out.rg)

    def dma(self, queue, out, in_, transpose=False):
        self.P.dma(queue, out.ap, in_.ap, reads=in_.rg, writes=out.rg,
                   transpose=transpose)



class Core:

    def __init__(self, nc, T=TCORE, arena_kb=68):
        self.nc = nc
        self.T = T
        self.P = Prog(nc)
        self.V = E(self.P)
        self._stack = contextlib.ExitStack()
        st = self._stack
        self.xres_h = st.enter_context(
            nc.sbuf_tensor("xres", [128, KT * T], F32))
        nbytes = arena_kb * 1024
        self.arena_h = st.enter_context(
            nc.sbuf_tensor("arena", [128, nbytes // 4], I32))
        self.ar = Arena(self.arena_h, nbytes)
        self.pb = []
        for i in range(8):
            h = st.enter_context(nc.psum_tensor(f"pb{i}", [128, 512], F32))
            self.pb.append(Tile(h[:, :], (f"P{i}",)))
        self.cst = {}

    def x(self, kt, t0, t1):
        ap = self.xres_h[:, kt * self.T + t0: kt * self.T + t1]
        rg = tuple(f"X{kt}_{s}" for s in range(t0 // 128, (t1 - 1) // 128 + 1))
        return Tile(ap, rg)

    def pbank(self, i, n, dtype=F32, off=0):
        t = self.pb[i]
        if dtype == F32:
            return Tile(t.ap[:, off:off + n], t.rg)
        ap = t.ap.bitcast(dtype)
        return Tile(ap[:, off:off + n], t.rg)

    def pbank_rows(self, i, p, n, dtype=F32, off=0):
        t = self.pbank(i, off + n, dtype)
        return Tile(t.ap[0:p, off:off + n], t.rg)

    def declare_const_inputs(self, names_shapes):
        out = {}
        for name, shape, dtype in names_shapes:
            ap = self.nc.dram_tensor(name, list(shape), dtype,
                                     kind="ExternalInput").ap()
            out[name] = Tile(ap, (f"D_{name}",))
        return out

    def load_const(self, dram_tile, shape, dtype, queue="qs"):
        t = carve(self.ar, shape[0] * shape[1] // 128 * 128 if False else
                  int(np.prod(shape[1:])) * 1, dtype)
        raise RuntimeError("unused")


def dram_in(nc, name, shape, dtype):
    return Tile(nc.dram_tensor(name, list(shape), dtype,
                               kind="ExternalInput").ap(), (f"D_{name}",))


def dram_out(nc, name, shape, dtype):
    return Tile(nc.dram_tensor(name, list(shape), dtype,
                               kind="ExternalOutput").ap(), (f"D_{name}",))


def dram_int(nc, name, shape, dtype):
    return Tile(nc.dram_tensor(name, list(shape), dtype,
                               kind="Internal").ap(), (f"D_{name}",))



def setup_common_consts(C, qn="qs"):
    V = C.V
    specs = [("cst_ones", (128, 128), BF16),
             ("cst_id_bf", (128, 128), BF16),
             ("cst_id_f32", (128, 128), F32),
             ("cst_rotp", (128, 128), BF16),
             ("cst_mask01", (128, 128), BF16)]
    dr = C.declare_const_inputs(specs)
    for name, shape, dtype in specs:
        t = carve(C.ar, shape[1], dtype)
        V.dma(qn, t, dr[name])
        C.cst[name[4:]] = t
    C.ones_col = C.cst["ones"][:, 0:1]
    C.ones_row = Tile(C.cst["ones"].ap[0:1, 0:128], C.cst["ones"].rg)


def prep_bias_cols(C, dram_t, n, qn="qs"):
    V = C.V
    a = n // 128
    stg = carve(C.ar, 128, F32)
    src = dram_t.view(dram_t.ap.rearrange("(a b) -> a b", b=128))
    V.dma(qn, Tile(stg.ap[0:a, :], stg.rg), src)
    ps = C.pbank(5, a)
    ident = Tile(C.cst["id_f32"].ap[0:a, 0:a], C.cst["id_f32"].rg)
    V.trans(ps, Tile(stg.ap[0:a, :], stg.rg), ident)
    col = carve(C.ar, a, F32)
    V.copy(col, ps)
    return col


def ln_stats(C, srcf, Tt, b_sum=6, b_bc0=6, b_bc1=7):
    V, ar = C.V, C.ar
    ar.push()
    xb = carve(ar, Tt, BF16)
    sq = carve(ar, Tt, BF16)
    sum_ps = C.pbank_rows(b_sum, 1, Tt)
    sq_ps = C.pbank_rows(7 if b_sum == 6 else 5, 1, Tt)
    for kt in range(KT):
        s = srcf(kt)
        C.P.op("act", lambda e, o=xb.ap, x=s.ap: e.copy(o, x),
               reads=s.rg, writes=xb.rg)
        V.mm(sum_ps, C.ones_col, xb, start=(kt == 0), stop=(kt == KT - 1))
        C.P.op("act", lambda e, o=sq.ap, x=s.ap: e.square(o, x),
               reads=s.rg, writes=sq.rg)
        V.mm(sq_ps, C.ones_col, sq, start=(kt == 0), stop=(kt == KT - 1))
    r_mu = carve(ar, Tt, F32)[0:1, :]
    r_e2 = carve(ar, Tt, F32)[0:1, :]
    r_t = carve(ar, Tt, F32)[0:1, :]
    r_rstd = carve(ar, Tt, F32)[0:1, :]
    rb_A = carve(ar, Tt, BF16)[0:1, :]
    rb_B = carve(ar, Tt, BF16)[0:1, :]
    V.ts(r_mu, sum_ps, 1.0 / D, None, ALU.mult)
    V.ts(r_e2, sq_ps, 1.0 / D, None, ALU.mult)
    V.tt(r_t, r_mu, r_mu, ALU.mult)
    V.tt(r_e2, r_e2, r_t, ALU.subtract)
    V.ts(r_e2, r_e2, EPS, None, ALU.add)
    V.recip(r_t, r_e2)
    C.P.op("act", lambda e, o=r_rstd.ap, x=r_t.ap:
           e.activation(o, x, AF.Sqrt), reads=r_t.rg, writes=r_rstd.rg)
    V.stt(r_t, r_mu, -1.0, r_rstd, ALU.mult, ALU.mult)
    V.copy(rb_A, r_rstd)
    V.copy(rb_B, r_t)
    A_bc = C.pbank(b_bc0, Tt)
    B_bc = C.pbank(b_bc1, Tt)
    V.mm(A_bc, C.ones_row, rb_A, start=True, stop=True)
    V.mm(B_bc, C.ones_row, rb_B, start=True, stop=True)
    ar.pop()
    return A_bc, B_bc


def ln_apply_to(C, A_bc, B_bc, srcf, dstf, Tt):
    V, ar = C.V, C.ar
    ar.push()
    t1 = carve(ar, Tt, F32)
    for kt in range(KT):
        s = srcf(kt)
        V.stt(t1, s, 1.0, A_bc, ALU.mult, ALU.mult)
        V.tt(dstf(kt), t1, B_bc, ALU.add)
    ar.pop()


def ln_residual_add(C, A_bc, B_bc, yf, t0, t1):
    V, ar = C.V, C.ar
    Tt = t1 - t0
    ar.push()
    tf = carve(ar, Tt, F32)
    for kt in range(KT):
        yk = yf[:, kt * Tt:(kt + 1) * Tt]
        V.stt(tf, yk, 1.0, A_bc, ALU.mult, ALU.mult)
        V.tt(tf, tf, B_bc, ALU.add)
        xk = C.x(kt, t0, t1)
        V.tt(xk, xk, tf, ALU.add)
    ar.pop()


def load_x_to_xres(C, xio, qn="qs"):
    V, ar = C.V, C.ar
    ar.push()
    stg = [carve(ar, 2048, F32) for _ in range(2)]
    for tt in range(C.T // 128):
        s = stg[tt % 2]
        V.dma(qn, s, xio[tt * 128:(tt + 1) * 128, :])
        for kt in range(KT):
            ps = C.pbank(4 + (kt % 2), 128)
            V.trans(ps, s[:, kt * 128:(kt + 1) * 128], C.cst["id_f32"])
            V.copy(C.x(kt, tt * 128, (tt + 1) * 128), ps)
    ar.pop()


def store_xres_to_out(C, out, qn="qs"):
    V, ar = C.V, C.ar
    ar.push()
    stg = [carve(ar, 2048, F32) for _ in range(2)]
    for tt in range(C.T // 128):
        s = stg[tt % 2]
        for kt in range(KT):
            ps = C.pbank(4 + (kt % 2), 128)
            V.trans(ps, C.x(kt, tt * 128, (tt + 1) * 128), C.cst["id_f32"])
            V.copy(s[:, kt * 128:(kt + 1) * 128], ps)
        V.dma(qn, out[tt * 128:(tt + 1) * 128, :], s)
    ar.pop()



def emit_l0_attn(C, W, TT=256):
    V, ar = C.V, C.ar
    for t0 in range(0, C.T, TT):
        Tt = min(TT, C.T - t0)
        t1 = t0 + Tt
        ar.push()
        ob = carve(ar, KT * Tt, BF16)
        ar.push()
        xln = carve(ar, KT * Tt, BF16)
        kb = carve(ar, KT * Tt, BF16)
        vb = carve(ar, KT * Tt, BF16)
        qb = carve(ar, KT * Tt, BF16)

        def xk(kt, t0=t0, t1=t1):
            return C.x(kt, t0, t1)

        A_bc, B_bc = ln_stats(C, xk, Tt)
        ln_apply_to(C, A_bc, B_bc, xk,
                    lambda kt: xln[:, kt * Tt:(kt + 1) * Tt], Tt)

        ar.push()
        ck = carve(ar, 4 * Tt, BF16)
        cv = carve(ar, 4 * Tt, BF16)
        for dst, wname, bias in ((ck, "Wd_k", W["bdk"]), (cv, "Wd_v", W["bdv"])):
            ar.push()
            wd = carve(ar, KT * 512, BF16)
            V.dma("qp", wd, W[wname].view(
                W[wname].ap.rearrange("(a p) l -> p a l", p=128)))
            for lt in range(4):
                ps = C.pbank(lt % 2, Tt)
                for kt in range(KT):
                    lhsT = wd[:, kt * 512 + lt * 128: kt * 512 + (lt + 1) * 128]
                    V.mm(ps, lhsT, xln[:, kt * Tt:(kt + 1) * Tt],
                         start=(kt == 0), stop=(kt == KT - 1))
                V.act(dst[:, lt * Tt:(lt + 1) * Tt], ps, AF.Identity,
                      bias=bias[:, lt:lt + 1])
            ar.pop()

        for dst, wname, src, bias, scl in (
                (kb, "Wu_k", ck, W["buk_s"], RSQD),
                (vb, "Wu_v", cv, W["buv"], 1.0)):
            ar.push()
            wu = carve(ar, 4 * D, BF16)
            V.dma("qp", wu, W[wname].view(
                W[wname].ap.rearrange("(a p) n -> p a n", p=128)))
            for nt in range(KT):
                ps = C.pbank(nt % 2, Tt)
                for lt in range(4):
                    lhsT = wu[:, lt * D + nt * 128: lt * D + (nt + 1) * 128]
                    V.mm(ps, lhsT, src[:, lt * Tt:(lt + 1) * Tt],
                         start=(lt == 0), stop=(lt == 3))
                V.act(dst[:, nt * Tt:(nt + 1) * Tt], ps, AF.Identity,
                      bias=bias[:, nt:nt + 1], scale=scl)
            ar.pop()
        ar.pop()

        ar.push()
        wqg = [carve(ar, KT * 128, BF16) for _ in range(2)]
        for nt in range(KT):
            wq = wqg[nt % 2]
            V.dma("qp", wq, W["Wq"].view(
                W["Wq"].ap[:, nt * 128:(nt + 1) * 128]
                .rearrange("(a p) n -> p a n", p=128)))
            ps = C.pbank(2 + nt % 2, Tt)
            for kt in range(KT):
                lhsT = wq[:, kt * 128:(kt + 1) * 128]
                V.mm(ps, lhsT, xln[:, kt * Tt:(kt + 1) * Tt],
                     start=(kt == 0), stop=(kt == KT - 1))
            V.act(qb[:, nt * Tt:(nt + 1) * Tt], ps, AF.Identity,
                  bias=W["bq"][:, nt:nt + 1])
        ar.pop()

        for sub in range(Tt // 128):
            ar.push()
            vT = carve(ar, D, BF16)
            at = carve(ar, 256, BF16)
            at3 = at.view(at.ap.rearrange("p (h g) -> p h g", h=16))
            ar.push()
            qT = carve(ar, D, BF16)
            kT = carve(ar, D, BF16)
            for dstT, src in ((qT, qb), (kT, kb), (vT, vb)):
                for nt in range(KT):
                    psb = C.pbank(4 + nt % 2, 128, BF16)
                    V.trans(psb, src[:, nt * Tt + sub * 128:
                                     nt * Tt + sub * 128 + 128], C.cst["id_bf"])
                    V.copy(dstT[:, nt * 128:(nt + 1) * 128], psb)
            sim = carve(ar, 256, F32)
            sim3 = sim.view(sim.ap.rearrange("p (h g) -> p h g", h=16))
            prod = carve(ar, D, BF16)
            prod3 = prod.view(prod.ap.rearrange("p (h d) -> p h d", h=16))
            qT3 = qT.view(qT.ap.rearrange("p (h d) -> p h d", h=16))
            for g in range(NH):
                kslice = kT[:, g * 128:(g + 1) * 128]
                kbc = kslice.view(kslice.ap.unsqueeze(1)
                                  .broadcast_to([128, 16, 128]))
                V.tt(prod3, qT3, kbc, ALU.mult)
                V.red(sim3[:, :, g:g + 1], prod3, ALU.add)
            mx = carve(ar, 16, F32)
            V.red(mx, sim3, ALU.max)
            mxb = mx.view(mx.ap.unsqueeze(2).broadcast_to([128, 16, 16]))
            ex = carve(ar, 256, F32)
            ex3 = ex.view(ex.ap.rearrange("p (h g) -> p h g", h=16))
            V.stt(ex3, sim3, 1.0, mxb, ALU.mult, ALU.subtract)
            V.act(ex, ex, AF.Exp)
            sm = carve(ar, 16, F32)
            V.red(sm, ex3, ALU.add)
            rc = carve(ar, 16, F32)
            V.recip(rc, sm)
            rcb = rc.view(rc.ap.unsqueeze(2).broadcast_to([128, 16, 16]))
            V.tt(at3, ex3, rcb, ALU.mult)
            ar.pop()
            outT = carve(ar, D, F32)
            outT3 = outT.view(outT.ap.rearrange("p (h d) -> p h d", h=16))
            tmp = carve(ar, D, BF16)
            tmp3 = tmp.view(tmp.ap.rearrange("p (h d) -> p h d", h=16))
            for g in range(NH):
                ab = at3[:, :, g:g + 1]
                ab = ab.view(ab.ap.broadcast_to([128, 16, 128]))
                vs = vT[:, g * 128:(g + 1) * 128]
                vb_ = vs.view(vs.ap.unsqueeze(1).broadcast_to([128, 16, 128]))
                if g == 0:
                    V.tt(outT3, ab, vb_, ALU.mult)
                else:
                    V.tt(tmp3, ab, vb_, ALU.mult)
                    V.tt(outT3, outT3, tmp3, ALU.add)
            outTb = carve(ar, D, BF16)
            V.copy(outTb, outT)
            for nt in range(KT):
                psb = C.pbank(4 + nt % 2, 128, BF16)
                V.trans(psb, outTb[:, nt * 128:(nt + 1) * 128], C.cst["id_bf"])
                V.copy(ob[:, nt * Tt + sub * 128: nt * Tt + sub * 128 + 128],
                       psb)
            ar.pop()
        ar.pop()

        ar.push()
        yv = carve(ar, KT * Tt, F32)
        ar.push()
        wog = [carve(ar, KT * 128, BF16) for _ in range(2)]
        for kt in range(KT):
            wo = wog[kt % 2]
            V.dma("qp", wo, W["Wo"].view(
                W["Wo"].ap[:, kt * 128:(kt + 1) * 128]
                .rearrange("(a p) n -> p a n", p=128)))
            ps = C.pbank(kt % 2, Tt)
            for nt in range(KT):
                lhsT = wo[:, nt * 128:(nt + 1) * 128]
                V.mm(ps, lhsT, ob[:, nt * Tt:(nt + 1) * Tt],
                     start=(nt == 0), stop=(nt == KT - 1))
            V.act(yv[:, kt * Tt:(kt + 1) * Tt], ps, AF.Identity,
                  bias=W["bo"][:, kt:kt + 1])
        ar.pop()
        A2, B2 = ln_stats(C, lambda kt: yv[:, kt * Tt:(kt + 1) * Tt], Tt)
        ln_residual_add(C, A2, B2, yv, t0, t1)
        ar.pop()
        ar.pop()
    assert ar.off <= 4096 + 16384, f"arena leak? off={ar.off}"



def emit_dequant_nvfp4(C, packed, scales, out_nat, n_rows, kpack, row0=0,
                       qn="qs", qo="qa", CK=1024):
    V, ar = C.V, C.ar
    for r0 in range(row0, row0 + n_rows, 128):
        for c0 in range(0, kpack, CK):
            nsc = CK * 2 // 32
            ar.push()
            pk = carve(ar, CK, I32)
            sc = carve(ar, nsc, F32)
            V.dma(qn, pk, packed[r0:r0 + 128, c0:c0 + CK])
            V.dma(qn, sc, scales[r0:r0 + 128, c0 * 2 // 32:(c0 + CK) * 2 // 32])
            lo32 = carve(ar, CK, I32)
            hi32 = carve(ar, CK, I32)
            V.ts(lo32, pk, 15, None, ALU.bitwise_and)
            V.ts(hi32, pk, 4, 15, ALU.logical_shift_right, ALU.bitwise_and)
            wt = carve(ar, 2 * CK, BF16)
            c = carve(ar, CK, BF16)
            g8 = carve(ar, CK, BF16)
            m = carve(ar, CK, BF16)
            sg = carve(ar, CK, BF16)
            v1 = carve(ar, CK, BF16)
            w2 = carve(ar, CK, BF16)
            scb = sc.view(sc.ap.unsqueeze(2).broadcast_to([128, nsc, 16]))
            wt2 = wt.view(wt.ap.rearrange("p (j two) -> p two j", two=2))
            for half, src32 in ((0, lo32), (1, hi32)):
                V.copy(c, src32)
                V.ts(g8, c, 8.0, 8.0, ALU.is_ge, ALU.mult)
                V.stt(m, c, 0.0, g8, ALU.add, ALU.subtract)
                V.ts(sg, g8, -0.25, 1.0, ALU.mult, ALU.add)
                V.ts(v1, m, 0.5, None, ALU.mult)
                V.stt(v1, m, -2.0, v1, ALU.add, ALU.max)
                V.stt(w2, m, -8.0, m, ALU.add, ALU.add)
                V.tt(v1, v1, w2, ALU.max)
                V.tt(v1, v1, sg, ALU.mult)
                ov = wt2[:, half, :]
                ov3 = ov.view(ov.ap.rearrange("p (blk i) -> p blk i", i=16))
                v13 = v1.view(v1.ap.rearrange("p (blk i) -> p blk i", i=16))
                V.tt(ov3, v13, scb, ALU.mult)
            V.dma(qo, out_nat[r0:r0 + 128, 2 * c0:2 * (c0 + CK)], wt)
            ar.pop()


def emit_dequant_4bit(C, q, scales, out_nat, n_rows, kdim, blk=64, row0=0,
                      qn="qs", qo="qa", CK=2048):
    V, ar = C.V, C.ar
    for r0 in range(row0, row0 + n_rows, 128):
        for c0 in range(0, kdim, CK):
            nsc = CK // blk
            ar.push()
            qk = carve(ar, CK, I32)
            sc = carve(ar, nsc, F32)
            V.dma(qn, qk, q[r0:r0 + 128, c0:c0 + CK])
            V.dma(qn, sc, scales[r0:r0 + 128, c0 // blk:(c0 + CK) // blk])
            c = carve(ar, CK, BF16)
            V.copy(c, qk)
            V.ts(c, c, -7.5, None, ALU.add)
            wt = carve(ar, CK, BF16)
            scb = sc.view(sc.ap.unsqueeze(2).broadcast_to([128, nsc, blk]))
            c3 = c.view(c.ap.rearrange("p (b i) -> p b i", i=blk))
            w3 = wt.view(wt.ap.rearrange("p (b i) -> p b i", i=blk))
            V.tt(w3, c3, scb, ALU.mult)
            V.dma(qo, out_nat[r0:r0 + 128, c0:c0 + CK], wt)
            ar.pop()



def emit_l0_ffn(C, wg_nat, wu_nat, wdn_nat, xln_d, inter_d, TT=512, N1=512):
    V, ar = C.V, C.ar
    T = C.T

    for t0 in range(0, T, TT):
        Tt = min(TT, T - t0)
        ar.push()
        xln = carve(ar, KT * Tt, BF16)
        A_bc, B_bc = ln_stats(C, lambda kt: C.x(kt, t0, t0 + Tt), Tt)
        ln_apply_to(C, A_bc, B_bc, lambda kt: C.x(kt, t0, t0 + Tt),
                    lambda kt: xln[:, kt * Tt:(kt + 1) * Tt], Tt)
        xv = xln.view(xln.ap.rearrange("p (k t) -> p k t", k=KT))
        dst = xln_d.view(xln_d.ap[:, :, t0:t0 + Tt].transpose([1, 0, 2]))
        V.dma("qa", dst, xv)
        ar.pop()

    nt1 = N1 // 128
    for n0 in range(0, MLP0, N1):
        ar.push()
        wgt = carve(ar, KT * N1, BF16)
        wut = carve(ar, KT * N1, BF16)
        for kt in range(KT):
            V.dma("qs", wgt[:, kt * N1:(kt + 1) * N1],
                  wg_nat[n0:n0 + N1, kt * 128:(kt + 1) * 128], transpose=True)
            V.dma("qs", wut[:, kt * N1:(kt + 1) * N1],
                  wu_nat[n0:n0 + N1, kt * 128:(kt + 1) * 128], transpose=True)
        for t0 in range(0, T, TT):
            Tt = min(TT, T - t0)
            ar.push()
            xln = carve(ar, KT * Tt, BF16)
            xsrc = xln_d.view(xln_d.ap[:, :, t0:t0 + Tt].transpose([1, 0, 2]))
            V.dma("qp", xln.view(xln.ap.rearrange("p (k t) -> p k t", k=KT)),
                  xsrc)
            g4 = carve(ar, nt1 * Tt, BF16)
            u4 = carve(ar, nt1 * Tt, BF16)
            for nt in range(nt1):
                for dst, wsrc, bank in ((g4, wgt, 0), (u4, wut, 1)):
                    ps = C.pbank(bank, Tt)
                    for kt in range(KT):
                        lhsT = wsrc[:, kt * N1 + nt * 128: kt * N1 + (nt + 1) * 128]
                        V.mm(ps, lhsT, xln[:, kt * Tt:(kt + 1) * Tt],
                             start=(kt == 0), stop=(kt == KT - 1))
                    V.act(dst[:, nt * Tt:(nt + 1) * Tt], ps, AF.Identity)
            sq = carve(ar, nt1 * Tt, BF16)
            t1 = carve(ar, nt1 * Tt, BF16)
            C.P.op("act", lambda e, o=sq.ap, x=g4.ap: e.square(o, x),
                   reads=g4.rg, writes=sq.rg)
            V.ts(t1, sq, GELU_A, 1.0, ALU.mult, ALU.add)
            V.tt(t1, g4, t1, ALU.mult)
            V.act(t1, t1, AF.Tanh, scale=GELU_C)
            V.ts(t1, t1, 0.5, 0.5, ALU.mult, ALU.add)
            V.tt(t1, g4, t1, ALU.mult)
            V.tt(t1, t1, u4, ALU.mult)
            iv = t1.view(t1.ap.rearrange("p (n t) -> p n t", n=nt1))
            dst = inter_d.view(
                inter_d.ap[n0:n0 + N1, t0:t0 + Tt]
                .rearrange("(n p) t -> p n t", p=128))
            V.dma("qa", dst, iv)
            ar.pop()
        ar.pop()

    NC2 = 512
    for t0 in range(0, T, TT):
        Tt = min(TT, T - t0)
        ar.push()
        yv = carve(ar, KT * Tt, F32)
        for kh in range(2):
            ar.push()
            wdt = [carve(ar, 4 * 1024, BF16) for _ in range(2)]
            itb = [carve(ar, 4 * Tt, BF16) for _ in range(2)]
            nchunks = MLP0 // NC2
            for ci in range(nchunks):
                n0 = ci * NC2
                wd = wdt[ci % 2]
                ib = itb[ci % 2]
                for a in range(NC2 // 128):
                    V.dma("qs", wd[:, a * 1024:(a + 1) * 1024],
                          wdn_nat[kh * 1024:(kh + 1) * 1024,
                                  n0 + a * 128:n0 + (a + 1) * 128],
                          transpose=True)
                    V.dma("qp", ib[:, a * Tt:(a + 1) * Tt],
                          inter_d[n0 + a * 128:n0 + (a + 1) * 128, t0:t0 + Tt])
                for a in range(NC2 // 128):
                    for kt in range(8):
                        ps = C.pbank(kt, Tt)
                        lhsT = wd[:, a * 1024 + kt * 128: a * 1024 + (kt + 1) * 128]
                        V.mm(ps, lhsT, ib[:, a * Tt:(a + 1) * Tt],
                             start=(ci == 0 and a == 0),
                             stop=(ci == nchunks - 1 and a == NC2 // 128 - 1))
            for kt in range(8):
                V.copy(yv[:, (kh * 8 + kt) * Tt:(kh * 8 + kt + 1) * Tt],
                       C.pbank(kt, Tt))
            ar.pop()
        A2, B2 = ln_stats(C, lambda kt: yv[:, kt * Tt:(kt + 1) * Tt], Tt)
        ln_residual_add(C, A2, B2, yv, t0, t0 + Tt)
        ar.pop()


def build_l0_ffn_launch(T=TCORE, shard=None):
    nc = bass.Bass("TRN2", target_bir_lowering=False)
    C = Core(nc, T)
    xio = dram_in(nc, "xio", [T, D], F32)
    xout = dram_out(nc, "xout", [T, D], F32)
    gp = dram_in(nc, "gate_W_packed", [MLP0, 1024], I32)
    gs = dram_in(nc, "gate_scales", [MLP0, 64], F32)
    up = dram_in(nc, "up_W_packed", [MLP0, 1024], I32)
    us = dram_in(nc, "up_scales", [MLP0, 64], F32)
    dp = dram_in(nc, "down_W_packed", [D, MLP0 // 2], I32)
    ds = dram_in(nc, "down_scales", [D, MLP0 // 32], F32)
    wg_nat = dram_int(nc, "wg_nat", [MLP0, D], BF16)
    wu_nat = dram_int(nc, "wu_nat", [MLP0, D], BF16)
    wdn_nat = dram_int(nc, "wdn_nat", [D, MLP0], BF16)
    xln_d = dram_int(nc, "xln_d", [KT, 128, T], BF16)
    inter_d = dram_int(nc, "inter_d", [MLP0, T], BF16)
    setup_common_consts(C)
    emit_dequant_nvfp4(C, gp, gs, wg_nat, MLP0, 1024)
    emit_dequant_nvfp4(C, up, us, wu_nat, MLP0, 1024)
    emit_dequant_nvfp4(C, dp, ds, wdn_nat, D, MLP0 // 2)
    load_x_to_xres(C, xio)
    emit_l0_ffn(C, wg_nat, wu_nat, wdn_nat, xln_d, inter_d)
    store_xres_to_out(C, xout)
    C.P.emit(tail=_tail_waits(C))
    return nc




def emit_l1_xln_spill(C, xln_d, TT=512):
    V, ar = C.V, C.ar
    for t0 in range(0, C.T, TT):
        Tt = min(TT, C.T - t0)
        ar.push()
        xln = carve(ar, KT * Tt, BF16)
        A_bc, B_bc = ln_stats(C, lambda kt: C.x(kt, t0, t0 + Tt), Tt)
        ln_apply_to(C, A_bc, B_bc, lambda kt: C.x(kt, t0, t0 + Tt),
                    lambda kt: xln[:, kt * Tt:(kt + 1) * Tt], Tt)
        xv = xln.view(xln.ap.rearrange("p (k t) -> p k t", k=KT))
        dst = xln_d.view(xln_d.ap[:, :, t0:t0 + Tt].transpose([1, 0, 2]))
        V.dma("qa", dst, xv)
        ar.pop()


def emit_l1_qkv(C, W, xln_d, spills, l1d, need_q=True):
    V, ar = C.V, C.ar
    T = C.T
    qhat_d, khat_d, khatT_d, vt_d = spills
    ar.push()
    sdq = carve(ar, NH * 128, BF16)
    sdk = carve(ar, NH * 128, BF16)
    V.dma("qs", sdq, l1d["cst_l1_sdq"])
    V.dma("qs", sdk, l1d["cst_l1_sdk"])
    TQ = min(256, T)
    nsc = TQ // 128
    for h in range(NH):
        ar.push()
        wq = carve(ar, KT * 128, BF16)
        wk = carve(ar, KT * 128, BF16)
        wv = carve(ar, KT * 128, BF16)
        if need_q:
            V.dma("qp", wq, W["W_Q"].view(
                W["W_Q"].ap[h].rearrange("(a p) n -> p a n", p=128)))
        V.dma("qp", wk, W["W_K"].view(
            W["W_K"].ap[h].rearrange("(a p) n -> p a n", p=128)))
        V.dma("qp", wv, W["W_V"].view(
            W["W_V"].ap[h].rearrange("(a p) n -> p a n", p=128)))
        for t0 in range(0, T, TQ):
            ar.push()
            xln = carve(ar, KT * TQ, BF16)
            xsrc = xln_d.view(xln_d.ap[:, :, t0:t0 + TQ].transpose([1, 0, 2]))
            V.dma("qs", xln.view(xln.ap.rearrange("p (k t) -> p k t", k=KT)),
                  xsrc)
            cq = carve(ar, TQ, BF16)
            sq_ = carve(ar, TQ, BF16)
            V.dma("qs", cq, l1d["cst_l1_cq"][:, t0:t0 + TQ])
            V.dma("qs", sq_, l1d["cst_l1_sq"][:, t0:t0 + TQ])
            sdqb = sdq[:, h * 128:(h + 1) * 128]
            sdqb = sdqb.view(sdqb.ap.unsqueeze(1).broadcast_to([128, nsc, 128]))
            sdkb = sdk[:, h * 128:(h + 1) * 128]
            sdkb = sdkb.view(sdkb.ap.unsqueeze(1).broadcast_to([128, nsc, 128]))
            for which, wsrc, sdb, dst_d in (
                    ("q", wq, sdqb, qhat_d), ("k", wk, sdkb, khat_d),
                    ("v", wv, None, vt_d)):
                if which == "q" and not need_q:
                    continue
                ps = C.pbank(0, TQ)
                for kt in range(KT):
                    V.mm(ps, wsrc[:, kt * 128:(kt + 1) * 128],
                         xln[:, kt * TQ:(kt + 1) * TQ],
                         start=(kt == 0), stop=(kt == KT - 1))
                if which == "v":
                    vb = carve(ar, TQ, BF16)
                    V.copy(vb, ps)
                    vt = carve(ar, TQ, BF16)
                    for s in range(nsc):
                        psb = C.pbank(2 + s % 2, 128, BF16)
                        V.trans(psb, vb[:, s * 128:(s + 1) * 128], C.cst["id_bf"])
                        V.copy(vt[:, s * 128:(s + 1) * 128], psb)
                    V.dma("qa", vt_d[h, :, t0:t0 + TQ], vt)
                else:
                    qs_ = carve(ar, TQ, BF16)
                    V.copy(qs_, ps)
                    pr = C.pbank(1, TQ)
                    V.mm(pr, C.cst["rotp"], qs_, start=True, stop=True)
                    t1 = carve(ar, TQ, BF16)
                    V.tt(t1, qs_, cq, ALU.mult)
                    t2 = carve(ar, TQ, BF16)
                    V.tt(t2, pr, sq_, ALU.mult)
                    V.tt(t1, t1, t2, ALU.add)
                    hat = carve(ar, TQ, BF16)
                    h3 = hat.view(hat.ap.rearrange("p (s r) -> p s r", r=128))
                    t13 = t1.view(t1.ap.rearrange("p (s r) -> p s r", r=128))
                    V.tt(h3, t13, sdb, ALU.mult)
                    V.dma("qa", dst_d[h, :, t0:t0 + TQ], hat)
                    if which == "k":
                        kt_ = carve(ar, TQ, BF16)
                        for s in range(nsc):
                            psb = C.pbank(2 + s % 2, 128, BF16)
                            V.trans(psb, hat[:, s * 128:(s + 1) * 128],
                                    C.cst["id_bf"])
                            V.copy(kt_[:, s * 128:(s + 1) * 128], psb)
                        V.dma("qa", khatT_d[h, :, t0:t0 + TQ], kt_)
            ar.pop()
        ar.pop()
    ar.pop()


def emit_l1_state_combine(C, s_all, S, Sb, coef_d):
    V, ar = C.V, C.ar
    coef = carve(ar, NCORES * NH, F32)
    V.dma("qs", coef, coef_d)
    V.memset(S, 0.0)
    ar.push()
    tmp = carve(ar, NH * 128, F32)
    tm3 = tmp.view(tmp.ap.rearrange("p (h v) -> p h v", h=NH))
    st = carve(ar, NH * 128, F32)
    st3 = st.view(st.ap.rearrange("p (h v) -> p h v", h=NH))
    for i in range(NCORES):
        V.dma("qs", st, s_all[i])
        cf = coef[:, i * NH:(i + 1) * NH]
        cfb = cf.view(cf.ap.unsqueeze(2).broadcast_to([128, NH, 128]))
        V.tt(tm3, st3, cfb, ALU.mult)
        V.tt(S, S, tmp, ALU.add)
    V.copy(Sb, S)
    ar.pop()


def emit_l1_retention(C, spills, yn_d, s_out, S, Sb, with_y=True):
    V, ar = C.V, C.ar
    T = C.T
    qhat_d, khat_d, khatT_d, vt_d = spills
    ar.push()
    dec = carve(ar, NH * 128, F32)
    V.dma("qs", dec, C.l1d["cst_l1_dec"])
    gnwx = gnbx = None
    if with_y:
        gnwx, gnbx = C.cst["gnwx"], C.cst["gnbx"]
    for u in range(T // 128):
        ar.push()
        qh = carve(ar, NH * 128, BF16)
        kh = carve(ar, NH * 128, BF16)
        khT = carve(ar, NH * 128, BF16)
        vt = carve(ar, NH * 128, BF16)
        sl = slice(u * 128, (u + 1) * 128)
        if with_y:
            V.dma("qs", qh.view(qh.ap.rearrange("p (h r) -> p h r", h=NH)),
                  qhat_d.view(qhat_d.ap[:, :, sl].transpose([1, 0, 2])))
            V.dma("qs", kh.view(kh.ap.rearrange("p (h r) -> p h r", h=NH)),
                  khat_d.view(khat_d.ap[:, :, sl].transpose([1, 0, 2])))
        V.dma("qs", khT.view(khT.ap.rearrange("p (h r) -> p h r", h=NH)),
              khatT_d.view(khatT_d.ap[:, :, sl].transpose([1, 0, 2])))
        V.dma("qs", vt.view(vt.ap.rearrange("p (h r) -> p h r", h=NH)),
              vt_d.view(vt_d.ap[:, :, sl].transpose([1, 0, 2])))
        if with_y:
            Y = carve(ar, NH * 128, F32)
        retm = carve(ar, 512, BF16)
        mb = C.cst["mask01"]
        mbb = mb.view(mb.ap.unsqueeze(1).broadcast_to([128, 4, 128]))
        for g in range(4):
            gsl = slice(g * 512, (g + 1) * 512)
            pkv = C.pbank(4 + g % 2, 512)
            if with_y:
                pr = C.pbank(g % 2, 512)
                py = C.pbank(2 + g % 2, 512)
                for a in range(4):
                    h = g * 4 + a
                    hs = slice(h * 128, (h + 1) * 128)
                    qs_ = slice(a * 128, (a + 1) * 128)
                    V.mm(Tile(pr.ap[:, qs_], pr.rg), kh[:, hs], qh[:, hs],
                         start=True, stop=True)
                r3 = retm.view(retm.ap.rearrange("p (a r) -> p a r", r=128))
                p3 = pr.view(pr.ap.rearrange("p (a r) -> p a r", r=128))
                V.tt(r3, p3, mbb, ALU.mult)
                for a in range(4):
                    h = g * 4 + a
                    hs = slice(h * 128, (h + 1) * 128)
                    qs_ = slice(a * 128, (a + 1) * 128)
                    V.mm(Tile(py.ap[:, qs_], py.rg), retm[:, qs_], vt[:, hs],
                         start=True, stop=False)
                    V.mm(Tile(py.ap[:, qs_], py.rg), qh[:, hs],
                         Sb[:, hs], start=False, stop=True)
                V.copy(Y[:, gsl], py)
            for a in range(4):
                h = g * 4 + a
                hs = slice(h * 128, (h + 1) * 128)
                qs_ = slice(a * 128, (a + 1) * 128)
                V.mm(Tile(pkv.ap[:, qs_], pkv.rg), khT[:, hs], vt[:, hs],
                     start=True, stop=True)
            V.tt(Tile(S.ap[:, gsl], S.rg), Tile(S.ap[:, gsl], S.rg), pkv,
                 ALU.add)
            V.tt(Tile(S.ap[:, gsl], S.rg), Tile(S.ap[:, gsl], S.rg),
                 dec[:, gsl], ALU.mult)
            V.copy(Tile(Sb.ap[:, gsl], Sb.rg), Tile(S.ap[:, gsl], S.rg))
        if with_y:
            Y3 = Y.view(Y.ap.rearrange("p (h v) -> p h v", h=NH))
            sm = carve(ar, NH, F32)
            V.red(sm, Y3, ALU.add)
            ysq = carve(ar, NH * 128, BF16)
            C.P.op("act", lambda e, o=ysq.ap, x=Y.ap: e.square(o, x),
                   reads=Y.rg, writes=ysq.rg)
            s2 = carve(ar, NH, F32)
            V.red(s2, ysq.view(ysq.ap.rearrange("p (h v) -> p h v", h=NH)),
                  ALU.add)
            V.ts(sm, sm, 1.0 / 128, None, ALU.mult)
            V.ts(s2, s2, 1.0 / 128, None, ALU.mult)
            mu2 = carve(ar, NH, F32)
            V.tt(mu2, sm, sm, ALU.mult)
            V.tt(s2, s2, mu2, ALU.subtract)
            V.ts(s2, s2, EPS, None, ALU.add)
            V.recip(s2, s2)
            rstd = carve(ar, NH, F32)
            C.P.op("act", lambda e, o=rstd.ap, x=s2.ap:
                   e.activation(o, x, AF.Sqrt), reads=s2.rg, writes=rstd.rg)
            smb = sm.view(sm.ap.unsqueeze(2).broadcast_to([128, NH, 128]))
            rsb = rstd.view(rstd.ap.unsqueeze(2).broadcast_to([128, NH, 128]))
            yn = ysq
            yn3 = yn.view(yn.ap.rearrange("p (h v) -> p h v", h=NH))
            V.stt(Y3, Y3, 1.0, smb, ALU.mult, ALU.subtract)
            V.tt(Y3, Y3, rsb, ALU.mult)
            V.tt(Y3, Y3, gnwx, ALU.mult)
            V.tt(yn3, Y3, gnbx, ALU.add)
            V.dma("qa", yn_d[sl, :], yn)
        ar.pop()
    ar.pop()
    V.dma("qa", s_out, S)


def emit_l1_gate_out(C, W, yn_d, TT=256):
    V, ar = C.V, C.ar
    for t0 in range(0, C.T, TT):
        Tt = min(TT, C.T - t0)
        t1 = t0 + Tt
        ar.push()
        xln = carve(ar, KT * Tt, BF16)
        A_bc, B_bc = ln_stats(C, lambda kt: C.x(kt, t0, t1), Tt)
        ln_apply_to(C, A_bc, B_bc, lambda kt: C.x(kt, t0, t1),
                    lambda kt: xln[:, kt * Tt:(kt + 1) * Tt], Tt)
        ynF = carve(ar, KT * Tt, BF16)
        for nt in range(KT):
            V.dma("qs", ynF[:, nt * Tt:(nt + 1) * Tt],
                  yn_d[t0:t1, nt * 128:(nt + 1) * 128], transpose=True)
        gy = carve(ar, KT * Tt, BF16)
        ar.push()
        wg_ = [carve(ar, KT * 128, BF16) for _ in range(2)]
        for nt in range(KT):
            w = wg_[nt % 2]
            V.dma("qp", w, W["W_G"].view(
                W["W_G"].ap[:, nt * 128:(nt + 1) * 128]
                .rearrange("(a p) n -> p a n", p=128)))
            ps = C.pbank(nt % 2, Tt)
            for kt in range(KT):
                V.mm(ps, w[:, kt * 128:(kt + 1) * 128],
                     xln[:, kt * Tt:(kt + 1) * Tt],
                     start=(kt == 0), stop=(kt == KT - 1))
            gs_ = carve(ar, Tt, BF16)
            V.act(gs_, ps, AF.Silu)
            V.tt(gy[:, nt * Tt:(nt + 1) * Tt], gs_,
                 ynF[:, nt * Tt:(nt + 1) * Tt], ALU.mult)
            ar.pop(); ar.push()
            wg_ = [carve(ar, KT * 128, BF16) for _ in range(2)]
        ar.pop()
        yv = carve(ar, KT * Tt, F32)
        ar.push()
        wo_ = [carve(ar, KT * 128, BF16) for _ in range(2)]
        for kt in range(KT):
            w = wo_[kt % 2]
            V.dma("qp", w, W["W_O"].view(
                W["W_O"].ap[:, kt * 128:(kt + 1) * 128]
                .rearrange("(a p) n -> p a n", p=128)))
            ps = C.pbank(2 + kt % 2, Tt)
            for nt in range(KT):
                V.mm(ps, w[:, nt * 128:(nt + 1) * 128],
                     gy[:, nt * Tt:(nt + 1) * Tt],
                     start=(nt == 0), stop=(nt == KT - 1))
            V.copy(yv[:, kt * Tt:(kt + 1) * Tt], ps)
        ar.pop()
        A2, B2 = ln_stats(C, lambda kt: yv[:, kt * Tt:(kt + 1) * Tt], Tt)
        ln_residual_add(C, A2, B2, yv, t0, t1)
        ar.pop()


def setup_l1_consts(C):
    specs = [("cst_l1_cq", (128, C.T), BF16),
             ("cst_l1_sq", (128, C.T), BF16),
             ("cst_l1_sdq", (128, NH * 128), BF16),
             ("cst_l1_sdk", (128, NH * 128), BF16),
             ("cst_l1_dec", (128, NH * 128), F32),
             ("cst_l1_coef", (128, NCORES * NH), F32)]
    C.l1d = C.declare_const_inputs(specs)
    return C.l1d


def prep_gn_rows(C, gnw_dram, gnb_dram):
    V, ar = C.V, C.ar
    for name, dr in (("gnwx", gnw_dram), ("gnbx", gnb_dram)):
        out = carve(ar, D, BF16)
        ar.push()
        row = carve(ar, D, BF16)
        V.dma("qp", Tile(row.ap[0:1, :], row.rg), dr.view(dr.ap.unsqueeze(0)))
        for q in range(4):
            ps = C.pbank(6, 512)
            V.mm(ps, C.ones_row, Tile(row.ap[0:1, q * 512:(q + 1) * 512], row.rg),
                 start=True, stop=True)
            V.copy(Tile(out.ap[:, q * 512:(q + 1) * 512], out.rg), ps)
        C.cst[name] = out
        ar.pop()


def host_l1_consts(core_id, T=TCORE):
    j = core_id % 4
    off = j * T
    d = np.arange(128)
    i = d // 2
    xscale = (2.0 * i + 0.4 * 128) / (1.4 * 128)
    inv_freq = 1.0 / 10000.0 ** (np.arange(64) / 64.0)
    pos = off + np.arange(T)
    ang = pos[None, :] * inv_freq[i][:, None]
    gammas = 1.0 - np.exp(np.linspace(np.log(1 / 32), np.log(1 / 512), 16))
    r = np.arange(128)
    sdq = np.zeros((128, NH * 128))
    sdk = np.zeros((128, NH * 128))
    dec = np.zeros((128, NH * 128))
    for h in range(NH):
        sc = xscale[:, None] ** (r[None, :] / 512.0)
        g = gammas[h] ** r[None, :]
        sdq[:, h * 128:(h + 1) * 128] = sc * g
        sdk[:, h * 128:(h + 1) * 128] = 1.0 / (sc * g)
        dec[:, h * 128:(h + 1) * 128] = (
            gammas[h] ** 128 * xscale[:, None] ** (128 / 512.0))
    coef = np.zeros((128, NCORES * NH))
    for src in range(NCORES):
        if src // 4 != core_id // 4:
            continue
        ij = src % 4
        if ij >= j:
            continue
        delta = (j - ij - 1) * T
        for h in range(NH):
            coef[:, src * NH + h] = (
                gammas[h] ** delta * xscale ** (delta / 512.0))
    return {
        "cst_l1_cq": np.cos(ang).astype(BB),
        "cst_l1_sq": np.sin(ang).astype(BB),
        "cst_l1_sdq": sdq.astype(BB),
        "cst_l1_sdk": sdk.astype(BB),
        "cst_l1_dec": dec.astype(np.float32),
        "cst_l1_coef": coef.astype(np.float32),
    }


def build_l1_state_launch(T=TCORE):
    nc = bass.Bass("TRN2", target_bir_lowering=False)
    C = Core(nc, T)
    xio = dram_in(nc, "xio", [T, D], F32)
    W = {"W_K": dram_in(nc, "W_K", [NH, D, 128], F32),
         "W_V": dram_in(nc, "W_V", [NH, D, 128], F32),
         "W_Q": dram_in(nc, "W_Q", [NH, D, 128], F32)}
    s_out = dram_out(nc, "s_out", [128, NH * 128], F32)
    xln_d = dram_int(nc, "xln_d", [KT, 128, T], BF16)
    qhat_d = dram_int(nc, "qhat_d", [NH, 128, T], BF16)
    khat_d = dram_int(nc, "khat_d", [NH, 128, T], BF16)
    khatT_d = dram_int(nc, "khatT_d", [NH, 128, T], BF16)
    vt_d = dram_int(nc, "vt_d", [NH, 128, T], BF16)
    setup_common_consts(C)
    l1d = setup_l1_consts(C)
    S = carve(C.ar, NH * 128, F32)
    Sb = carve(C.ar, NH * 128, BF16)
    C.V.memset(S, 0.0)
    load_x_to_xres(C, xio)
    emit_l1_xln_spill(C, xln_d)
    emit_l1_qkv(C, W, xln_d, (qhat_d, khat_d, khatT_d, vt_d), l1d, need_q=False)
    emit_l1_retention(C, (qhat_d, khat_d, khatT_d, vt_d), None, s_out, S, Sb,
                      with_y=False)
    C.P.emit(tail=_tail_waits(C))
    return nc


def build_l1_full_launch(T=TCORE):
    nc = bass.Bass("TRN2", target_bir_lowering=False)
    C = Core(nc, T)
    xio = dram_in(nc, "xio", [T, D], F32)
    xout = dram_out(nc, "xout", [T, D], F32)
    W = {"W_K": dram_in(nc, "W_K", [NH, D, 128], F32),
         "W_V": dram_in(nc, "W_V", [NH, D, 128], F32),
         "W_Q": dram_in(nc, "W_Q", [NH, D, 128], F32),
         "W_G": dram_in(nc, "W_G", [D, D], F32),
         "W_O": dram_in(nc, "W_O", [D, D], F32)}
    gnw = dram_in(nc, "gn_weight", [D], F32)
    gnb = dram_in(nc, "gn_bias", [D], F32)
    s_all = dram_in(nc, "s_all", [NCORES, 128, NH * 128], F32)
    xln_d = dram_int(nc, "xln_d", [KT, 128, T], BF16)
    qhat_d = dram_int(nc, "qhat_d", [NH, 128, T], BF16)
    khat_d = dram_int(nc, "khat_d", [NH, 128, T], BF16)
    khatT_d = dram_int(nc, "khatT_d", [NH, 128, T], BF16)
    vt_d = dram_int(nc, "vt_d", [NH, 128, T], BF16)
    yn_d = dram_int(nc, "yn_d", [T, D], BF16)
    s_out = dram_out(nc, "s_out2", [128, NH * 128], F32)
    setup_common_consts(C)
    l1d = setup_l1_consts(C)
    C.ar.push()
    prep_gn_rows(C, gnw, gnb)
    S = carve(C.ar, NH * 128, F32)
    Sb = carve(C.ar, NH * 128, BF16)
    load_x_to_xres(C, xio)
    emit_l1_xln_spill(C, xln_d)
    emit_l1_qkv(C, W, xln_d, (qhat_d, khat_d, khatT_d, vt_d), l1d, need_q=True)
    emit_l1_state_combine(C, s_all, S, Sb, l1d["cst_l1_coef"])
    emit_l1_retention(C, (qhat_d, khat_d, khatT_d, vt_d), yn_d, s_out, S, Sb,
                      with_y=True)
    C.ar.pop()
    emit_l1_gate_out(C, W, yn_d)
    store_xres_to_out(C, xout)
    C.P.emit(tail=_tail_waits(C))
    return nc



def emit_l1_ffn(C, wup_nat, wdn1_nat, W, xln_d, x2_d, TT=512, TT2=256, N1=512):
    V, ar = C.V, C.ar
    T = C.T

    u_all = carve(ar, T, BF16)
    ar.push()
    la = carve(ar, KT * 16, BF16)
    V.dma("qp", la, W["lora_a"].view(
        W["lora_a"].ap.rearrange("(a p) r -> p a r", p=128)))
    for t0 in range(0, T, TT):
        Tt = min(TT, T - t0)
        ar.push()
        xln = carve(ar, KT * Tt, BF16)
        A_bc, B_bc = ln_stats(C, lambda kt: C.x(kt, t0, t0 + Tt), Tt)
        ln_apply_to(C, A_bc, B_bc, lambda kt: C.x(kt, t0, t0 + Tt),
                    lambda kt: xln[:, kt * Tt:(kt + 1) * Tt], Tt)
        xv = xln.view(xln.ap.rearrange("p (k t) -> p k t", k=KT))
        dst = xln_d.view(xln_d.ap[:, :, t0:t0 + Tt].transpose([1, 0, 2]))
        V.dma("qa", dst, xv)
        pu = C.pbank_rows(5, 16, Tt)
        for kt in range(KT):
            V.mm(pu, la[:, kt * 16:(kt + 1) * 16],
                 xln[:, kt * Tt:(kt + 1) * Tt],
                 start=(kt == 0), stop=(kt == KT - 1))
        V.copy(Tile(u_all.ap[0:16, t0:t0 + Tt], u_all.rg), pu)
        ar.pop()
    ar.pop()

    vb = carve(ar, T, BF16)
    nt1 = N1 // 128
    ar.push()
    for n0 in range(0, MLP1, N1):
        ar.push()
        wut = carve(ar, KT * N1, BF16)
        lb = carve(ar, nt1 * 128, BF16)
        da = carve(ar, nt1 * 16, BF16)
        for kt in range(KT):
            V.dma("qs", wut[:, kt * N1:(kt + 1) * N1],
                  wup_nat[n0:n0 + N1, kt * 128:(kt + 1) * 128], transpose=True)
        V.dma("qp", Tile(lb.ap[0:16, :], lb.rg),
              W["lora_b"][:, n0:n0 + N1])
        V.dma("qp", da, W["down_a"].view(
            W["down_a"].ap[n0:n0 + N1, :].rearrange("(a p) r -> p a r", p=128)))
        for t0 in range(0, T, TT):
            Tt = min(TT, T - t0)
            ar.push()
            xln = carve(ar, KT * Tt, BF16)
            xsrc = xln_d.view(xln_d.ap[:, :, t0:t0 + Tt].transpose([1, 0, 2]))
            V.dma("qp", xln.view(xln.ap.rearrange("p (k t) -> p k t", k=KT)),
                  xsrc)
            x2t = carve(ar, nt1 * Tt, BF16)
            for nt in range(nt1):
                ps = C.pbank(nt % 2, Tt)
                for kt in range(KT):
                    V.mm(ps, wut[:, kt * N1 + nt * 128: kt * N1 + (nt + 1) * 128],
                         xln[:, kt * Tt:(kt + 1) * Tt],
                         start=(kt == 0), stop=False)
                V.mm(ps, Tile(lb.ap[0:16, nt * 128:(nt + 1) * 128], lb.rg),
                     Tile(u_all.ap[0:16, t0:t0 + Tt], u_all.rg),
                     start=False, stop=True)
                V.act(x2t[:, nt * Tt:(nt + 1) * Tt], ps, AF.Relu,
                      bias=W["bup"][:, (n0 // 128 + nt):(n0 // 128 + nt) + 1])
            dst = x2_d.view(
                x2_d.ap[n0:n0 + N1, t0:t0 + Tt]
                .rearrange("(n p) t -> p n t", p=128))
            V.dma("qa", dst, x2t.view(x2t.ap.rearrange("p (n t) -> p n t",
                                                       n=nt1)))
            pv = C.pbank_rows(4 + t0 // TT if T > TT else 4, 16, Tt)
            for nt in range(nt1):
                V.mm(pv, da[:, nt * 16:(nt + 1) * 16],
                     x2t[:, nt * Tt:(nt + 1) * Tt],
                     start=(n0 == 0 and nt == 0),
                     stop=(n0 == MLP1 - N1 and nt == nt1 - 1))
            ar.pop()
        ar.pop()
    for t0 in range(0, T, TT):
        pv = C.pbank_rows(4 + t0 // TT if T > TT else 4, 16, min(TT, T - t0))
        V.copy(Tile(vb.ap[0:16, t0:t0 + min(TT, T - t0)], vb.rg), pv)
    if DBG_L1FFN is not None:
        V.dma("qa", DBG_L1FFN[0], Tile(vb.ap[0:16, :], vb.rg))
    ar.pop()

    ar.push()
    db = carve(ar, KT * 128, BF16)
    V.dma("qp", Tile(db.ap[0:16, :], db.rg), W["down_b"])
    NC2 = 512
    for t0 in range(0, T, TT2):
        Tt = min(TT2, T - t0)
        ar.push()
        yv = carve(ar, KT * Tt, F32)
        ar.push()
        wdt = [carve(ar, 4 * 1024, BF16) for _ in range(2)]
        x2b = [carve(ar, 4 * Tt, BF16) for _ in range(2)]
        nchunks = MLP1 // NC2
        for kh in range(2):
            for ci in range(nchunks):
                n0 = ci * NC2
                wd = wdt[ci % 2]
                xb = x2b[ci % 2]
                for a in range(NC2 // 128):
                    V.dma("qs", wd[:, a * 1024:(a + 1) * 1024],
                          wdn1_nat[kh * 1024:(kh + 1) * 1024,
                                   n0 + a * 128:n0 + (a + 1) * 128],
                          transpose=True)
                    V.dma("qp", xb[:, a * Tt:(a + 1) * Tt],
                          x2_d[n0 + a * 128:n0 + (a + 1) * 128, t0:t0 + Tt])
                for a in range(NC2 // 128):
                    for k8 in range(8):
                        kt = kh * 8 + k8
                        ps = C.pbank(k8, Tt)
                        V.mm(ps, wd[:, a * 1024 + k8 * 128:
                                    a * 1024 + (k8 + 1) * 128],
                             xb[:, a * Tt:(a + 1) * Tt],
                             start=(ci == 0 and a == 0), stop=False)
            for k8 in range(8):
                kt = kh * 8 + k8
                ps = C.pbank(k8, Tt)
                V.mm(ps, Tile(db.ap[0:16, kt * 128:(kt + 1) * 128], db.rg),
                     Tile(vb.ap[0:16, t0:t0 + Tt], vb.rg),
                     start=False, stop=True)
                V.act(yv[:, kt * Tt:(kt + 1) * Tt], ps, AF.Identity,
                      bias=W["bdn"][:, kt:kt + 1])
        ar.pop()
        A2, B2 = ln_stats(C, lambda kt: yv[:, kt * Tt:(kt + 1) * Tt], Tt)
        ln_residual_add(C, A2, B2, yv, t0, t0 + Tt)
        ar.pop()
    ar.pop()


def build_l1_ffn_launch(T=TCORE):
    nc = bass.Bass("TRN2", target_bir_lowering=False)
    C = Core(nc, T)
    xio = dram_in(nc, "xio", [T, D], F32)
    xout = dram_out(nc, "xout", [T, D], F32)
    uq = dram_in(nc, "w_up_q", [MLP1, D], I32)
    usc = dram_in(nc, "w_up_scale", [MLP1, D // 64], F32)
    dq = dram_in(nc, "w_down_q", [D, MLP1], I32)
    dsc = dram_in(nc, "w_down_scale", [D, MLP1 // 64], F32)
    W = {"lora_a": dram_in(nc, "w_up_lora_a", [D, 16], F32),
         "lora_b": dram_in(nc, "w_up_lora_b", [16, MLP1], F32),
         "down_a": dram_in(nc, "w_down_lora_a", [MLP1, 16], F32),
         "down_b": dram_in(nc, "w_down_lora_b", [16, D], F32)}
    bup = dram_in(nc, "b_up", [MLP1], F32)
    bdn = dram_in(nc, "b_down", [D], F32)
    wup_nat = dram_int(nc, "wup_nat", [MLP1, D], BF16)
    wdn1_nat = dram_int(nc, "wdn1_nat", [D, MLP1], BF16)
    xln_d = dram_int(nc, "xln_d", [KT, 128, T], BF16)
    x2_d = dram_int(nc, "x2_d", [MLP1, T], BF16)
    setup_common_consts(C)
    W["bup"] = prep_bias_cols(C, bup, MLP1)
    W["bdn"] = prep_bias_cols(C, bdn, D)
    emit_dequant_4bit(C, uq, usc, wup_nat, MLP1, D)
    emit_dequant_4bit(C, dq, dsc, wdn1_nat, D, MLP1)
    load_x_to_xres(C, xio)
    emit_l1_ffn(C, wup_nat, wdn1_nat, W, xln_d, x2_d)
    store_xres_to_out(C, xout)
    C.P.emit(tail=_tail_waits(C))
    return nc



def _tail_waits(C):
    cnt = dict(C.P.counts)

    def mk(q):
        def f(eng, sems):
            for s in range(QSEMS):
                n = cnt[q] // QSEMS + (1 if s < cnt[q] % QSEMS else 0)
                if n:
                    eng.wait_ge(sems[f"{q}{s}"], 16 * n)
        return f
    return {"sync": mk("qs"), "scalar": mk("qa"), "gpsimd": mk("qp")}


def host_common_consts():
    return {
        "cst_ones": np.ones((128, 128), dtype=BB),
        "cst_id_bf": np.eye(128, dtype=BB),
        "cst_id_f32": np.eye(128, dtype=np.float32),
        "cst_rotp": _rotp_np().astype(BB),
        "cst_mask01": np.triu(np.ones((128, 128), dtype=np.float32)).astype(BB),
    }


def _rotp_np():
    p = np.zeros((128, 128), dtype=np.float32)
    for i in range(0, 128, 2):
        p[i + 1, i] = -1.0
        p[i, i + 1] = 1.0
    return p


def build_l0_attn_launch(T=TCORE):
    nc = bass.Bass("TRN2", target_bir_lowering=False)
    C = Core(nc, T)
    xio = dram_in(nc, "xio", [T, D], F32)
    xout = dram_out(nc, "xout", [T, D], F32)
    W = {}
    for n, s in [("Wd_k", [D, LAT]), ("Wu_k", [LAT, D]), ("Wd_v", [D, LAT]),
                 ("Wu_v", [LAT, D]), ("Wq", [D, D]), ("Wo", [D, D])]:
        W[n] = dram_in(nc, n, s, F32)
    for n in ("bd_k", "bd_v", "bu_k", "bu_v", "bq", "bo"):
        W[n] = dram_in(nc, n, [LAT if n in ("bd_k", "bd_v") else D], F32)
    setup_common_consts(C)
    W["bdk"] = prep_bias_cols(C, W["bd_k"], LAT)
    W["bdv"] = prep_bias_cols(C, W["bd_v"], LAT)
    buk = prep_bias_cols(C, W["bu_k"], D)
    W["buk_s"] = carve(C.ar, KT, F32)
    C.V.ts(W["buk_s"], buk, RSQD, None, ALU.mult)
    W["buv"] = prep_bias_cols(C, W["bu_v"], D)
    W["bq"] = prep_bias_cols(C, W["bq"], D)
    W["bo"] = prep_bias_cols(C, W["bo"], D)
    load_x_to_xres(C, xio)
    emit_l0_attn(C, W)
    store_xres_to_out(C, xout)
    C.P.emit(tail=_tail_waits(C))
    return nc



def _shard_x(x):
    out = []
    for c in range(NCORES):
        b, j = c // 4, c % 4
        out.append(np.ascontiguousarray(x[b, j * TCORE:(j + 1) * TCORE, :]))
    return out


def kernel(**ins):
    common = host_common_consts()
    l1c = [host_l1_consts(c) for c in range(NCORES)]
    xs = _shard_x(np.asarray(ins["L0_attn_x"], dtype=np.float32))

    nc = build_l0_attn_launch()
    base = {
        "Wd_k": ins["L0_attn_Wd_k"], "Wu_k": ins["L0_attn_Wu_k"],
        "Wd_v": ins["L0_attn_Wd_v"], "Wu_v": ins["L0_attn_Wu_v"],
        "Wq": ins["L0_attn_Wq"], "Wo": ins["L0_attn_Wo"],
        "bd_k": ins["L0_attn_bd_k"], "bd_v": ins["L0_attn_bd_v"],
        "bu_k": ins["L0_attn_bu_k"], "bu_v": ins["L0_attn_bu_v"],
        "bq": ins["L0_attn_bq"], "bo": ins["L0_attn_bo"],
    }
    base.update(common)
    maps = [dict(base, xio=xs[c]) for c in range(NCORES)]
    res = run_bass_kernel_spmd(nc, maps, core_ids=list(range(NCORES)))
    xs = [res.results[c]["xout"] for c in range(NCORES)]

    nc = build_l0_ffn_launch()
    base = {
        "gate_W_packed": ins["L0_ffn_gate_W_packed"],
        "gate_scales": ins["L0_ffn_gate_scales"],
        "up_W_packed": ins["L0_ffn_up_W_packed"],
        "up_scales": ins["L0_ffn_up_scales"],
        "down_W_packed": ins["L0_ffn_down_W_packed"],
        "down_scales": ins["L0_ffn_down_scales"],
    }
    base.update(common)
    maps = [dict(base, xio=xs[c]) for c in range(NCORES)]
    res = run_bass_kernel_spmd(nc, maps, core_ids=list(range(NCORES)))
    xs = [res.results[c]["xout"] for c in range(NCORES)]

    nc = build_l1_state_launch()
    base = {"W_Q": ins["L1_attn_W_Q"], "W_K": ins["L1_attn_W_K"],
            "W_V": ins["L1_attn_W_V"]}
    base.update(common)
    maps = [dict(base, xio=xs[c], **l1c[c]) for c in range(NCORES)]
    res = run_bass_kernel_spmd(nc, maps, core_ids=list(range(NCORES)))
    s_all = np.ascontiguousarray(
        np.stack([res.results[c]["s_out"] for c in range(NCORES)], axis=0))

    nc = build_l1_full_launch()
    base = {"W_Q": ins["L1_attn_W_Q"], "W_K": ins["L1_attn_W_K"],
            "W_V": ins["L1_attn_W_V"], "W_G": ins["L1_attn_W_G"],
            "W_O": ins["L1_attn_W_O"], "gn_weight": ins["L1_attn_gn_weight"],
            "gn_bias": ins["L1_attn_gn_bias"], "s_all": s_all}
    base.update(common)
    maps = [dict(base, xio=xs[c], **l1c[c]) for c in range(NCORES)]
    res = run_bass_kernel_spmd(nc, maps, core_ids=list(range(NCORES)))
    xs = [res.results[c]["xout"] for c in range(NCORES)]

    nc = build_l1_ffn_launch()
    base = {
        "w_up_q": ins["L1_ffn_w_up_q"], "w_up_scale": ins["L1_ffn_w_up_scale"],
        "w_down_q": ins["L1_ffn_w_down_q"],
        "w_down_scale": ins["L1_ffn_w_down_scale"],
        "w_up_lora_a": ins["L1_ffn_w_up_lora_a"],
        "w_up_lora_b": ins["L1_ffn_w_up_lora_b"],
        "w_down_lora_a": ins["L1_ffn_w_down_lora_a"],
        "w_down_lora_b": ins["L1_ffn_w_down_lora_b"],
        "b_up": ins["L1_ffn_b_up"], "b_down": ins["L1_ffn_b_down"],
    }
    base.update(common)
    maps = [dict(base, xio=xs[c]) for c in range(NCORES)]
    res = run_bass_kernel_spmd(nc, maps, core_ids=list(range(NCORES)))
    xs = [res.results[c]["xout"] for c in range(NCORES)]

    out = np.empty((B, S, D), dtype=np.float32)
    for c in range(NCORES):
        b, j = c // 4, c % 4
        out[b, j * TCORE:(j + 1) * TCORE, :] = xs[c]
    return out
